# version 10
# baseline (speedup 1.0000x reference)
import numpy as np
import concourse.bass as bass
import concourse.mybir as mybir
from concourse.bass_utils import run_bass_kernel_spmd

F32 = mybir.dt.float32
BF16 = mybir.dt.bfloat16
U8 = mybir.dt.uint8
AF = mybir.ActivationFunctionType
ALU = mybir.AluOpType
AX = mybir.AxisListType.X

B, T, C = 4, 2048, 2048
NH, HS = 16, 128
NCC = C // 128
TOK = 1024
DEXP = 1024
NEXP = 18
SCALE = float(1.0 / np.sqrt(HS))
BIAS_PASS = -4.0
BIAS_BLOCK = -30000.0
LN_EPS = 1e-5

ARENA_BYTES = 193536
OFF_X = 0
OFF_W = 65536
OFF_R = 65536
OFF_KVQ = 114688
OFF_WDN = 131072
OFF_Y = 155648
OFF_MID = 163840
OFF_ROWS = 16384
OFF_G = 180224
OFF_GEL = 184320
OFF_SQ = 180224
OFF_RB = 184320
OFF_X1B = 0
OFF_WUP = 32768
OFF_ATT = 188416
OFF_BC = 191488


class Sem:
    def __init__(self, handle):
        self.h = handle
        self.n = 0

    def inc(self, op, k=1):
        op.then_inc(self.h, k)
        self.n += k
        return self.n


def build_program(upto="full"):
    nc = bass.Bass("TRN2", target_bir_lowering=False)

    def din(name, shape, dtype=F32):
        return nc.dram_tensor(name, shape, dtype, kind="ExternalInput").ap()

    x_fm_d = din("xfm", [C, T])
    x_own_d = din("xown", [C, TOK])
    watT_d = din("watT", [C, 3 * C])
    wpT_d = din("wpT", [C, C])
    wshup_d = din("wshup", [2, C, DEXP])
    wshdn_d = din("wshdn", [2, DEXP, C])
    wrtup_d = din("wrtup", [16, C, DEXP])
    wrtdn_d = din("wrtdn", [16, DEXP, C])
    wr_d = din("wr", [C, 16])
    biastab_d = din("biastab", [128, 32])
    trimask_d = din("trimask", [128, 128])
    ident_d = din("ident", [128, 128])
    oh_d = din("oh", [16, 16 * 128])
    out_d = nc.dram_tensor("out", [C, TOK], F32, kind="ExternalOutput").ap()

    from contextlib import ExitStack
    es = ExitStack()
    arena = es.enter_context(nc.sbuf_tensor("arena", [128, ARENA_BYTES], U8))

    def bfv(off, shape):
        nb = int(np.prod(shape)) * 2
        ap = arena[:, off:off + nb].bitcast(BF16)
        if len(shape) > 1:
            ap = ap.rearrange("p (a b) -> p a b", a=shape[0])
        return ap

    def f32v(off, shape):
        nb = int(np.prod(shape)) * 4
        ap = arena[:, off:off + nb].bitcast(F32)
        if len(shape) > 1:
            ap = ap.rearrange("p (a b) -> p a b", a=shape[0])
        return ap

    x_fm = bfv(OFF_X, [NCC, T])
    wq = [bfv(OFF_W + 24576 * i, [NCC, 256]) for i in range(2)]
    wk = [bfv(OFF_W + 24576 * i + 8192, [NCC, 256]) for i in range(2)]
    wv = [bfv(OFF_W + 24576 * i + 16384, [NCC, 256]) for i in range(2)]
    k_fm = [bfv(OFF_KVQ + 20480 * i, [2, T]) for i in range(2)]
    v_tm = [bfv(OFF_KVQ + 20480 * i + 8192, [16, 256]) for i in range(2)]
    q_fm = [bfv(OFF_KVQ + 20480 * i + 16384, [2, TOK]) for i in range(2)]
    y_fm = bfv(OFF_Y, [NCC, TOK])
    att = [bfv(OFF_ATT + i * 1024, [512]) for i in range(3)]
    bcast_sb = f32v(OFF_BC, [512])
    wp = bfv(OFF_X, [NCC, C])
    r_st = f32v(OFF_R, [NCC, TOK])
    x1b = bfv(OFF_X1B, [NCC, TOK])
    wup = bfv(OFF_WUP, [NCC, DEXP])
    wdn = bfv(OFF_WDN, [8, C])
    mid = bfv(OFF_MID, [8, TOK])
    g_sb = f32v(OFF_G, [TOK])
    gel = [f32v(OFF_GEL + i * 2048, [512]) for i in range(2)]
    rows = f32v(OFF_ROWS, [6144])
    sq_rot1 = [bfv(131072 + i * 2048, [TOK]) for i in range(2)]
    rb_rot1 = [bfv(135168 + i * 2048, [TOK]) for i in range(2)]
    sq_rot2 = [bfv(0 + i * 2048, [TOK]) for i in range(2)]
    rb_rot2 = [bfv(4096 + i * 2048, [TOK]) for i in range(2)]
    rstd_sb = f32v(OFF_SQ, [TOK])
    mrs_sb = f32v(OFF_RB, [TOK])

    if True:
        ec = es.enter_context
        trimask = ec(nc.sbuf_tensor("s_trimask", [128, 128], F32))
        biastab = ec(nc.sbuf_tensor("s_biastab", [128, 32], F32))
        ident = ec(nc.sbuf_tensor("s_ident", [128, 128], F32))
        oh = ec(nc.sbuf_tensor("s_oh", [16, 16 * 128], F32))
        wrsb = ec(nc.sbuf_tensor("s_wrsb", [128, NCC, 16], F32))
        gatefm = ec(nc.sbuf_tensor("s_gatefm", [16, TOK], F32))
        onescb = ec(nc.sbuf_tensor("s_onescb", [128, 1], BF16))
        onesrf = ec(nc.sbuf_tensor("s_onesrf", [1, 128], F32))
        ptm8 = ec(nc.sbuf_tensor("s_ptm8", [128, 8, 16], F32))
        e16_8 = ec(nc.sbuf_tensor("s_e16_8", [128, 8, 16], F32))
        gtm8 = ec(nc.sbuf_tensor("s_gtm8", [128, 8, 16], F32))
        nm8 = ec(nc.sbuf_tensor("s_nm8", [128, 8], F32))
        epsv = ec(nc.sbuf_tensor("s_epsv", [128, 1], F32))
        sm1 = ec(nc.sbuf_tensor("s_sm1", [128, 1], F32))
        m8 = ec(nc.sbuf_tensor("s_m8", [128, 8], F32))
        thr = ec(nc.sbuf_tensor("s_thr", [128, 1], F32))
        srow_t = ec(nc.sbuf_tensor("s_srow", [128, 512], F32))
        WD, SD, PE, AC, VE, ST = (Sem(ec(nc.semaphore(s))) for s in
                                  ("WD", "SD", "PE", "AC", "VE", "ST"))
        XB = [Sem(ec(nc.semaphore(f"XB{i}"))) for i in range(4)]
        WG = [Sem(ec(nc.semaphore(f"WG{i}"))) for i in range(2)]
        WPS = Sem(ec(nc.semaphore("WPS")))
        WUS = Sem(ec(nc.semaphore("WUS")))
        WDS = Sem(ec(nc.semaphore("WDS")))
        SXQ = [Sem(ec(nc.semaphore(f"SXQ{i}"))) for i in range(4)]
        GPC = Sem(ec(nc.semaphore("GPC")))
        WK0 = Sem(ec(nc.semaphore("WK0")))
        WV0 = Sem(ec(nc.semaphore("WV0")))
        WQ0 = Sem(ec(nc.semaphore("WQ0")))
        TE, VEg, SE, GP, SY = nc.tensor, nc.vector, nc.scalar, nc.gpsimd, nc.sync

        pb = [es.enter_context(nc.psum_tensor(f"pb{i}", [128, 512], F32))
              for i in range(8)]

        def load_act(idx):
            SE.add_instruction(mybir.InstLoadActFuncSet(
                name=nc.get_next_instruction_name(),
                act_func_set_id=idx, ins=[], outs=[]))

        load_act(6)
        VEg.memset(onescb[:], 1.0)
        VEg.memset(epsv[:], LN_EPS)
        VE.inc(VEg.memset(onesrf[:], 1.0))
        ve_memset = VE.n

        SD.inc(SY.dma_start(trimask[:], trimask_d), 16)
        SD.inc(SY.dma_start(biastab[:], biastab_d), 16)
        SD.inc(SY.dma_start(ident[:], ident_d), 16)
        SD.inc(SY.dma_start(oh[:], oh_d), 16)
        SD.inc(SY.dma_start(
            wrsb[:, :, :],
            wr_d.rearrange("(a p) b -> p a b", p=128)), 16)
        sd_consts = SD.n
        VEg.wait_ge(SD.h, sd_consts)
        SE.wait_ge(SD.h, sd_consts)
        TE.wait_ge(VE.h, ve_memset)

        wd_wg = []
        wd_xb = [0] * 4

        def dma_wgroup(g):
            buf = g % 2
            for dst, base in ((wq[buf], 0), (wk[buf], C), (wv[buf], 2 * C)):
                WG[buf].inc(GP.dma_start(
                    dst[:, :, :],
                    watT_d.rearrange("(a p) o -> p a o", p=128)
                    [:, :, base + 256 * g: base + 256 * g + 256]), 16)
            wd_wg.append(WG[buf].n)

        def dma_x(xb):
            XB[xb].inc(GP.dma_start(
                x_fm[:, :, xb * 512:xb * 512 + 512],
                x_fm_d.rearrange("(a p) t -> p a t", p=128)
                [:, :, xb * 512:xb * 512 + 512]), 16)

        watT_r = watT_d.rearrange("(a p) o -> p a o", p=128)
        WK0.inc(GP.dma_start(wk[0][:, :, :], watT_r[:, :, C:C + 256]), 16)
        dma_x(0)
        WV0.inc(GP.dma_start(wv[0][:, :, :], watT_r[:, :, 2 * C:2 * C + 256]), 16)
        dma_x(1)
        WQ0.inc(GP.dma_start(wq[0][:, :, :], watT_r[:, :, 0:256]), 16)
        dma_x(2)
        dma_x(3)
        wd_wg.append(0)
        dma_wgroup_early = [False]

        NG = 8
        pe_qkv_done = [0] * NG
        pe_att_done = [0] * NG
        ac_att_done = [0] * NG
        bank_free_ve = {}
        exp_ac_of_bank = {0: 0, 1: 0, 2: 0}
        sumav_pe_of_att = {0: 0, 1: 0, 2: 0}
        sig_ve_free = {3: 0, 4: 0}
        av_ve_free = {5: 0, 6: 0}
        bc_ve_free = [0]
        deferred = []
        srow_pe_guard = [0]
        qkv_ti = [0]

        def qkv_group(g):
            buf = g % 2
            if g == 0:
                TE.wait_ge(WK0.h, 16)
            else:
                TE.wait_ge(WG[g % 2].h, wd_wg[g])
            if g >= 2:
                TE.wait_ge(AC.h, AC.n)
            if g == 0:
                TE.wait_ge(WV0.h, 16) if False else None
            xwait = set()

            def one_tile(lhs_fn, rhs_fn, evac_to, need_xb, n_k=NCC):
                if need_xb not in xwait:
                    TE.wait_ge(XB[need_xb].h, 16)
                    xwait.add(need_xb)
                bank = qkv_ti[0] % 3
                if bank_free_ve.get(bank, 0) > 0:
                    TE.wait_ge(VE.h, bank_free_ve[bank])
                w = evac_to.shape[-1]
                for ccx in range(n_k):
                    op = TE.matmul(pb[bank][:, 0:w], lhs_fn(ccx), rhs_fn(ccx),
                                   start=(ccx == 0), stop=(ccx == n_k - 1))
                PE.inc(op)
                VEg.wait_ge(PE.h, PE.n)
                VE.inc(VEg.tensor_copy(evac_to, pb[bank][:, 0:w]))
                bank_free_ve[bank] = VE.n
                qkv_ti[0] += 1

            for tb in range(4):
                for hr in range(2):
                    one_tile(lambda cc, hr=hr: wk[buf][:, cc, hr * 128:hr * 128 + 128],
                             lambda cc, tb=tb: x_fm[:, cc, tb * 512:tb * 512 + 512],
                             k_fm[buf][:, hr, tb * 512:tb * 512 + 512], tb)
                if g == 0 and tb == 0:
                    TE.wait_ge(WV0.h, 16)
                for tc in range(4 * tb, 4 * tb + 4):
                    one_tile(lambda cc, tc=tc: x_fm[:, cc, tc * 128:tc * 128 + 128],
                             lambda cc: wv[buf][:, cc, 0:256],
                             v_tm[buf][:, tc, 0:256], tb)
            if g == 0:
                TE.wait_ge(WQ0.h, 16)
            for hr in range(2):
                for tb in range(2):
                    one_tile(lambda cc, hr=hr: wq[buf][:, cc, hr * 128:hr * 128 + 128],
                             lambda cc, tb=tb: x_fm[:, cc, 1024 + tb * 512:1536 + tb * 512],
                             q_fm[buf][:, hr, tb * 512:tb * 512 + 512], 2 + tb)
            pe_qkv_done[g] = PE.n
            if g < NG - 1:
                if g + 1 == 1:
                    dma_wgroup(1)
                else:
                    GP.wait_ge(PE.h, pe_qkv_done[g])
                    dma_wgroup(g + 1)

        def att_group(g):
            buf = g % 2
            TE.wait_ge(VE.h, VE.n)
            tasks = []
            for hr in range(2):
                for qb in range(2):
                    slots = [0, 1, 2, 3, 8, 9, 10, 11] if qb == 0 else list(range(16))
                    for si, s in enumerate(slots):
                        tasks.append((hr, qb, s, si == 0, si == len(slots) - 1))
            L = len(tasks)

            def sc_emit(j):
                hr, qb, s, first, last = tasks[j]
                d = s - (8 + 4 * qb)
                diag = 0 <= d < 4
                coloff = 128 * d if diag else 0
                N = 512 - coloff
                bank = j % 3
                if exp_ac_of_bank[bank] > 0:
                    TE.wait_ge(AC.h, exp_ac_of_bank[bank])
                PE.inc(TE.matmul(pb[bank][:, coloff:coloff + N],
                                 k_fm[buf][:, hr, s * 128:s * 128 + 128],
                                 q_fm[buf][:, hr, qb * 512 + coloff:qb * 512 + coloff + N],
                                 start=True, stop=True))
                pe_sc = PE.n
                if diag:
                    VEg.wait_ge(PE.h, pe_sc)
                    VE.inc(VEg.tensor_tensor(pb[bank][:, coloff:coloff + 128],
                                             pb[bank][:, coloff:coloff + 128],
                                             trimask[:], ALU.add))
                    SE.wait_ge(VE.h, VE.n)
                else:
                    SE.wait_ge(PE.h, pe_sc)
                ab = j % 3
                if sumav_pe_of_att[ab] > 0:
                    SE.wait_ge(PE.h, sumav_pe_of_att[ab])
                AC.inc(SE.activation(att[ab][:, coloff:coloff + N],
                                     pb[bank][:, coloff:coloff + N],
                                     AF.Exp,
                                     bias=biastab[:, qb * 16 + s:qb * 16 + s + 1],
                                     scale=SCALE))
                exp_ac_of_bank[bank] = AC.n
                return AC.n

            def sumav_emit(j, ac_exp):
                hr, qb, s, first, last = tasks[j]
                d = s - (8 + 4 * qb)
                diag = 0 <= d < 4
                coloff = 128 * d if diag else 0
                N = 512 - coloff
                i2 = (g * 2 + hr) * 2 + qb
                sig_b = 3 + (i2 % 2)
                av_b = 5 + (i2 % 2)
                ab = j % 3
                TE.wait_ge(AC.h, ac_exp)
                if first:
                    if sig_ve_free[sig_b] > 0:
                        TE.wait_ge(VE.h, sig_ve_free[sig_b])
                    if av_ve_free[av_b] > 0:
                        TE.wait_ge(VE.h, av_ve_free[av_b])
                TE.matmul(pb[sig_b][0:1, coloff:coloff + N], onescb[:],
                          att[ab][:, coloff:coloff + N], start=first, stop=last)
                PE.inc(TE.matmul(pb[av_b][:, coloff:coloff + N],
                                 v_tm[buf][:, s, hr * 128:hr * 128 + 128],
                                 att[ab][:, coloff:coloff + N],
                                 start=first, stop=last))
                sumav_pe_of_att[ab] = PE.n
                if last:
                    pe_done = PE.n
                    VEg.wait_ge(PE.h, max(pe_done, srow_pe_guard[0]))
                    VE.inc(VEg.reciprocal(srow_t[0:1, 0:512],
                                          pb[sig_b][0:1, 0:512]))
                    sig_ve_free[sig_b] = VE.n
                    ve_recip = VE.n

                    def fin(sig_b=sig_b, av_b=av_b, hr=hr, qb=qb,
                            ve_recip=ve_recip):
                        TE.wait_ge(VE.h, ve_recip)
                        if bc_ve_free[0] > 0:
                            TE.wait_ge(VE.h, bc_ve_free[0])
                        PE.inc(TE.matmul(pb[7][:, 0:512], onesrf[:],
                                         srow_t[0:1, 0:512],
                                         start=True, stop=True))
                        VEg.wait_ge(PE.h, PE.n)
                        VE.inc(VEg.tensor_copy(bcast_sb[:, 0:512],
                                               pb[7][:, 0:512]))
                        bc_ve_free[0] = VE.n
                        VEg.wait_ge(VE.h, VE.n)
                        VE.inc(VEg.tensor_tensor(
                            y_fm[:, g * 2 + hr, qb * 512:qb * 512 + 512],
                            pb[av_b][:, 0:512], bcast_sb[:, 0:512], ALU.mult))
                        av_ve_free[av_b] = VE.n
                    deferred.append(fin)

            LA = 3
            pend = []
            fin_delay = [0]
            for j in range(L):
                if deferred:
                    fin_delay[0] += 1
                    if fin_delay[0] >= 4:
                        deferred.pop(0)()
                        srow_pe_guard[0] = PE.n
                        fin_delay[0] = 0
                if len(pend) == LA:
                    jj, ac = pend.pop(0)
                    sumav_emit(jj, ac)
                pend.append((j, sc_emit(j)))

            def drain():
                while pend:
                    jj, ac = pend.pop(0)
                    sumav_emit(jj, ac)
                pe_att_done[g] = PE.n
                ac_att_done[g] = AC.n
            return drain

        qkv_group(0)
        qkv_group(1)
        for g in range(NG):
            dr = att_group(g)
            dr()
            if g + 2 < NG:
                qkv_group(g + 2)
        while deferred:
            deferred.pop(0)()
            srow_pe_guard[0] = PE.n

        GP.wait_ge(PE.h, pe_qkv_done[NG - 1])
        WPS.inc(GP.dma_start(wp[:, :, :],
                             wpT_d.rearrange("(a p) o -> p a o", p=128)), 16)
        for q4 in range(4):
            SY.wait_ge(PE.h, pe_qkv_done[NG - 1] if q4 < 3
                       else pe_att_done[NG - 2])
            SXQ[q4].inc(SY.dma_start(
                r_st[:, q4 * 4:q4 * 4 + 4, :],
                x_own_d.rearrange("(a p) t -> p a t", p=128)[:, q4 * 4:q4 * 4 + 4, :]), 16)

        apply_ve_gate = [0]

        def mk_ln_stats(rb_rot, sq_rot):
            pe_s_free = {0: 0, 1: 0}

            def ln_stats_chunk(cc, first):
                par = cc % 2
                VEg.wait_ge(VE.h, VE.n)
                if pe_s_free[par] > 0:
                    VEg.wait_ge(PE.h, pe_s_free[par])
                VE.inc(VEg.tensor_copy(rb_rot[par][:, 0:TOK], r_st[:, cc, :]))
                VEg.wait_ge(VE.h, VE.n)
                VE.inc(VEg.tensor_tensor(sq_rot[par][:, 0:TOK],
                                         rb_rot[par][:, 0:TOK],
                                         rb_rot[par][:, 0:TOK], ALU.mult))
                ve_c = VE.n
                TE.wait_ge(VE.h, ve_c)
                for tb in range(2):
                    TE.matmul(pb[4 + tb][0:1, 0:512], onescb[:],
                              rb_rot[par][:, tb * 512:tb * 512 + 512],
                              start=first, stop=(cc == NCC - 1))
                    op = TE.matmul(pb[6 + tb][0:1, 0:512], onescb[:],
                                   sq_rot[par][:, tb * 512:tb * 512 + 512],
                                   start=first, stop=(cc == NCC - 1))
                PE.inc(op)
                pe_s_free[par] = PE.n
            return ln_stats_chunk

        def ln_rows_bcast():
            row0 = rows[0:1, 0:1024]
            row1 = rows[0:1, 1024:2048]
            row2 = rows[0:1, 2048:3072]
            VEg.wait_ge(PE.h, PE.n)
            VEg.tensor_copy(row0[0:1, 0:512], pb[4][0:1, 0:512])
            VEg.tensor_copy(row0[0:1, 512:1024], pb[5][0:1, 0:512])
            VEg.tensor_copy(row1[0:1, 0:512], pb[6][0:1, 0:512])
            VE.inc(VEg.tensor_copy(row1[0:1, 512:1024], pb[7][0:1, 0:512]))
            VEg.wait_ge(VE.h, VE.n)
            VE.inc(VEg.tensor_scalar(row0, row0, 1.0 / C, None, ALU.mult))
            VEg.wait_ge(VE.h, VE.n)
            VE.inc(VEg.tensor_tensor(row2, row0, row0, ALU.mult))
            VEg.wait_ge(VE.h, VE.n)
            VE.inc(VEg.scalar_tensor_tensor(row1, row1, 1.0 / C, row2,
                                            ALU.mult, ALU.subtract))
            ve_rows = VE.n
            load_act(6)
            SE.wait_ge(VE.h, ve_rows)
            AC.inc(SE.activation(row1, row1, AF.Ln, bias=epsv[0:1, 0:1]))
            SE.wait_ge(AC.h, AC.n)
            AC.inc(SE.activation(row1, row1, AF.Exp, scale=-0.5))
            ac_rstd = AC.n
            VEg.wait_ge(AC.h, ac_rstd)
            VE.inc(VEg.tensor_tensor(row2, row0, row1, ALU.mult))
            ve_mrs = VE.n
            TE.wait_ge(VE.h, ve_mrs)
            for tb in range(2):
                TE.matmul(pb[0 + tb][:, 0:512], onesrf[:],
                          row1[0:1, tb * 512:tb * 512 + 512], start=True, stop=True)
                op = TE.matmul(pb[2 + tb][:, 0:512], onesrf[:],
                               row2[0:1, tb * 512:tb * 512 + 512], start=True, stop=True)
            PE.inc(op)
            VEg.wait_ge(PE.h, PE.n)
            VEg.tensor_copy(rstd_sb[:, 0:512], pb[0][:, 0:512])
            VEg.tensor_copy(rstd_sb[:, 512:1024], pb[1][:, 0:512])
            VEg.tensor_copy(mrs_sb[:, 0:512], pb[2][:, 0:512])
            VE.inc(VEg.tensor_copy(mrs_sb[:, 512:1024], pb[3][:, 0:512]))
            apply_ve_gate[0] = VE.n
            VEg.wait_ge(VE.h, VE.n)

        def ln_apply_chunk(cc, cast_x1=False, store_out=False, router=False,
                           on_gp=False):
            if on_gp:
                GP.wait_ge(VE.h, apply_ve_gate[0])
                GPC.inc(GP.tensor_tensor(r_st[:, cc, :], r_st[:, cc, :],
                                         rstd_sb[:, 0:TOK], ALU.mult))
                GP.wait_ge(GPC.h, GPC.n)
                GPC.inc(GP.tensor_tensor(r_st[:, cc, :], r_st[:, cc, :],
                                         mrs_sb[:, 0:TOK], ALU.subtract))
                gp_sub = GPC.n
                if cast_x1:
                    VEg.wait_ge(GPC.h, gp_sub)
                    VE.inc(VEg.tensor_copy(x1b[:, cc, :], r_st[:, cc, :]))
                if router:
                    TE.wait_ge(GPC.h, gp_sub)
                    for tc in range(8):
                        op = TE.matmul(pb[6][:, tc * 16:tc * 16 + 16],
                                       r_st[:, cc, tc * 128:tc * 128 + 128],
                                       wrsb[:, cc, :],
                                       start=(cc == 0 and tc == 0),
                                       stop=(cc == NCC - 1 and tc == 7))
                    PE.inc(op)
                if store_out:
                    SY.wait_ge(GPC.h, gp_sub)
                    ST.inc(SY.dma_start(
                        out_d.rearrange("(a p) t -> p a t", p=128)[:, cc, :],
                        r_st[:, cc, :]), 16)
                return
            VE.inc(VEg.tensor_tensor(r_st[:, cc, :], r_st[:, cc, :],
                                     rstd_sb[:, 0:TOK], ALU.mult))
            VEg.wait_ge(VE.h, VE.n)
            VE.inc(VEg.tensor_tensor(r_st[:, cc, :], r_st[:, cc, :],
                                     mrs_sb[:, 0:TOK], ALU.subtract))
            ve_sub = VE.n
            VEg.wait_ge(VE.h, ve_sub)
            if cast_x1:
                VE.inc(VEg.tensor_copy(x1b[:, cc, :], r_st[:, cc, :]))
            if router:
                TE.wait_ge(VE.h, ve_sub)
                for tc in range(8):
                    op = TE.matmul(pb[6][:, tc * 16:tc * 16 + 16],
                                   r_st[:, cc, tc * 128:tc * 128 + 128],
                                   wrsb[:, cc, :],
                                   start=(cc == 0 and tc == 0),
                                   stop=(cc == NCC - 1 and tc == 7))
                PE.inc(op)
            if store_out:
                SY.wait_ge(VE.h, ve_sub)
                ST.inc(SY.dma_start(
                    out_d.rearrange("(a p) t -> p a t", p=128)[:, cc, :],
                    r_st[:, cc, :]), 16)

        TE.wait_ge(WPS.h, 16)
        TE.wait_ge(AC.h, AC.n)
        TE.wait_ge(VE.h, VE.n)
        proj_free = {0: 0, 1: 0}
        xq_waited = set()
        for oc in range(NCC):
            for tb in range(2):
                bank = (oc * 2 + tb) % 2
                if proj_free[bank] > 0:
                    TE.wait_ge(VE.h, proj_free[bank])
                for cc in range(NCC):
                    op = TE.matmul(pb[bank][:, 0:512],
                                   wp[:, cc, oc * 128:oc * 128 + 128],
                                   y_fm[:, cc, tb * 512:tb * 512 + 512],
                                   start=(cc == 0), stop=(cc == NCC - 1))
                PE.inc(op)
                VEg.wait_ge(PE.h, PE.n)
                if (oc // 4) not in xq_waited:
                    VEg.wait_ge(SXQ[oc // 4].h, 16)
                    xq_waited.add(oc // 4)
                VE.inc(VEg.tensor_tensor(r_st[:, oc, tb * 512:tb * 512 + 512],
                                         pb[bank][:, 0:512],
                                         r_st[:, oc, tb * 512:tb * 512 + 512],
                                         ALU.add))
                proj_free[bank] = VE.n
            if oc == 0:
                ln1_stats = mk_ln_stats(rb_rot1, sq_rot1)
            else:
                ln1_stats(oc - 1, first=(oc == 1))
        ln1_stats(NCC - 1, first=False)
        ve_proj_done = VE.n
        pe_proj_done = PE.n

        if upto == "y":
            GP.wait_ge(VE.h, VE.n)
            for cc in range(NCC):
                ST.inc(GP.dma_start(
                    out_d.rearrange("(a p) t -> p a t", p=128)[:, cc, :],
                    y_fm[:, cc, :]), 16)
            GP.wait_ge(ST.h, ST.n)
            es.close()
            return nc
        if upto == "resid":
            SY.wait_ge(VE.h, ve_proj_done)
            for cc in range(NCC):
                ST.inc(SY.dma_start(
                    out_d.rearrange("(a p) t -> p a t", p=128)[:, cc, :],
                    r_st[:, cc, :]), 16)
            SY.wait_ge(ST.h, ST.n)
            es.close()
            return nc

        GP.wait_ge(PE.h, pe_proj_done)
        wd_wup = [0] * NEXP
        wd_wdn = [0] * NEXP

        def dma_wup(e):
            srcw = wshup_d[e] if e < 2 else wrtup_d[e - 2]
            WUS.inc(GP.dma_start(wup[:, :, :],
                                 srcw.rearrange("(a p) d -> p a d", p=128)), 16)
            wd_wup[e] = WUS.n

        def dma_wdn(e):
            srcw = wshdn_d[e] if e < 2 else wrtdn_d[e - 2]
            WDS.inc(GP.dma_start(wdn[:, :, :],
                                 srcw.rearrange("(a p) h -> p a h", p=128)), 16)
            wd_wdn[e] = WDS.n

        dma_wup(0)
        dma_wdn(0)

        ln_rows_bcast()
        for cc in range(NCC):
            ln_apply_chunk(cc, cast_x1=True, router=True,
                           store_out=(upto == "x1"))
        ve_ln1 = VE.n
        pe_router = PE.n
        if upto == "x1":
            SY.wait_ge(ST.h, ST.n)
            es.close()
            return nc

        VEg.wait_ge(PE.h, pe_router)
        for tc in range(8):
            VE.inc(VEg.tensor_copy(ptm8[:, tc, :], pb[6][:, tc * 16:tc * 16 + 16]))
        VEg.wait_ge(VE.h, VE.n)
        for tc in range(8):
            VE.inc(VEg.reduce_max(nm8[:, tc:tc + 1], ptm8[:, tc, :], axis=AX,
                                  negate=True))
        ve_v1 = VE.n
        SE.wait_ge(VE.h, ve_v1)
        for tc in range(8):
            AC.inc(SE.activation(e16_8[:, tc, :], ptm8[:, tc, :], AF.Exp,
                                 bias=nm8[:, tc:tc + 1]))
        ac_exp8 = AC.n
        load_act(10)
        VEg.wait_ge(AC.h, ac_exp8)
        for tc in range(8):
            VE.inc(VEg.reduce_sum(sm1[:], e16_8[:, tc, :], axis=AX))
            VEg.wait_ge(VE.h, VE.n)
            VE.inc(VEg.reciprocal(sm1[:], sm1[:]))
            VEg.wait_ge(VE.h, VE.n)
            VE.inc(VEg.tensor_scalar(ptm8[:, tc, :], e16_8[:, tc, :], sm1[:],
                                     None, ALU.mult))
            VEg.wait_ge(VE.h, VE.n)
            VE.inc(VEg.max(m8[:], ptm8[:, tc, :]))
            VEg.wait_ge(VE.h, VE.n)
            VE.inc(VEg.tensor_copy(thr[:], m8[:, 1:2]))
            VEg.wait_ge(VE.h, VE.n)
            VE.inc(VEg.scalar_tensor_tensor(gtm8[:, tc, :], ptm8[:, tc, :],
                                            thr[:], ptm8[:, tc, :],
                                            ALU.is_ge, ALU.mult))
            VEg.wait_ge(VE.h, VE.n)
        ve_chains = VE.n

        nexp_run = NEXP
        if upto.startswith("ffn") and len(upto) > 3:
            nexp_run = int(upto[3:])
        gel_ve_free = {0: 0, 1: 0}
        up_ac_free = {0: 0, 1: 0}
        dn_ve_free = {2: 0, 3: 0}
        g_ve_free = [0]
        ve_gate_done = [0]
        gate_emitted = [False]

        def emit_gate_transposes():
            TE.wait_ge(VE.h, ve_chains)
            for tc in range(8):
                gb = 2 + (tc // 4)
                if dn_ve_free[gb] > 0:
                    TE.wait_ge(VE.h, dn_ve_free[gb])
                PE.inc(TE.transpose(pb[gb][0:16, (tc % 4) * 128:(tc % 4) * 128 + 128],
                                    gtm8[:, tc, :], ident[:]))
                if tc % 4 == 3:
                    VEg.wait_ge(PE.h, PE.n)
                    VE.inc(VEg.tensor_copy(
                        gatefm[0:16, (tc // 4) * 512:(tc // 4) * 512 + 512],
                        pb[gb][0:16, 0:512]))
            dn_ve_free[2] = VE.n
            dn_ve_free[3] = VE.n
            ve_gate_done[0] = VE.n
            gate_emitted[0] = True

        if upto == "gate":
            emit_gate_transposes()
            SY.wait_ge(VE.h, VE.n)
            ST.inc(SY.dma_start(out_d[0:16, :], gatefm[0:16, 0:TOK]), 16)
            SY.wait_ge(ST.h, ST.n)
            es.close()
            return nc

        for e in range(nexp_run):
            routed = e >= 2
            ve_gready = 0
            if routed:
                if not gate_emitted[0]:
                    emit_gate_transposes()
                re = e - 2
                TE.wait_ge(VE.h, max(ve_gate_done[0], g_ve_free[0]))
                for tb in range(2):
                    op = TE.matmul(pb[4 + tb][:, 0:512],
                                   oh[:, re * 128:re * 128 + 128],
                                   gatefm[0:16, tb * 512:tb * 512 + 512],
                                   start=True, stop=True)
                PE.inc(op)
                VEg.wait_ge(PE.h, PE.n)
                VEg.tensor_copy(g_sb[:, 0:512], pb[4][:, 0:512])
                VE.inc(VEg.tensor_copy(g_sb[:, 512:1024], pb[5][:, 0:512]))
                g_ve_free[0] = VE.n
                ve_gready = VE.n
            TE.wait_ge(WUS.h, wd_wup[e])
            mid_ready_tb = [0, 0]
            for tb in range(2):
                for dc in range(8):
                    bank = dc % 2
                    if up_ac_free[bank] > 0:
                        TE.wait_ge(AC.h, up_ac_free[bank])
                    for hc in range(NCC):
                        op = TE.matmul(pb[bank][:, 0:512],
                                       wup[:, hc, dc * 128:dc * 128 + 128],
                                       x1b[:, hc, tb * 512:tb * 512 + 512],
                                       start=(hc == 0), stop=(hc == NCC - 1))
                    PE.inc(op)
                    SE.wait_ge(PE.h, PE.n)
                    if routed:
                        gb2 = dc % 2
                        if gel_ve_free[gb2] > 0:
                            SE.wait_ge(VE.h, gel_ve_free[gb2])
                        AC.inc(SE.activation(gel[gb2][:, 0:512], pb[bank][:, 0:512],
                                             AF.Gelu))
                        up_ac_free[bank] = AC.n
                        VEg.wait_ge(AC.h, AC.n)
                        if dc == 0 and tb == 0:
                            VEg.wait_ge(VE.h, ve_gready)
                        VE.inc(VEg.tensor_tensor(
                            mid[:, dc, tb * 512:tb * 512 + 512],
                            gel[gb2][:, 0:512],
                            g_sb[:, tb * 512:tb * 512 + 512], ALU.mult))
                        gel_ve_free[gb2] = VE.n
                        mid_ready_tb[tb] = VE.n
                    else:
                        AC.inc(SE.activation(mid[:, dc, tb * 512:tb * 512 + 512],
                                             pb[bank][:, 0:512], AF.Gelu))
                        up_ac_free[bank] = AC.n
                        mid_ready_tb[tb] = AC.n
            pe_up_done = PE.n
            if e < nexp_run - 1:
                GP.wait_ge(PE.h, pe_up_done)
                dma_wup(e + 1)
            TE.wait_ge(WDS.h, wd_wdn[e])
            last_e = (e == nexp_run - 1)
            for tb in range(2):
                if routed:
                    TE.wait_ge(VE.h, mid_ready_tb[tb])
                else:
                    TE.wait_ge(AC.h, mid_ready_tb[tb])
                for hc in range(NCC):
                    bank = 2 + hc % 2
                    if dn_ve_free[bank] > 0:
                        TE.wait_ge(VE.h, dn_ve_free[bank])
                    for dc in range(8):
                        op = TE.matmul(pb[bank][:, 0:512],
                                       wdn[:, dc, hc * 128:hc * 128 + 128],
                                       mid[:, dc, tb * 512:tb * 512 + 512],
                                       start=(dc == 0), stop=(dc == 7))
                    PE.inc(op)
                    VEg.wait_ge(PE.h, PE.n)
                    VE.inc(VEg.tensor_tensor(
                        r_st[:, hc, tb * 512:tb * 512 + 512],
                        pb[bank][:, 0:512],
                        r_st[:, hc, tb * 512:tb * 512 + 512], ALU.add))
                    dn_ve_free[bank] = VE.n
                    if last_e and tb == 1:
                        if hc == 0:
                            ln2_stats = mk_ln_stats(rb_rot2, sq_rot2)
                        else:
                            ln2_stats(hc - 1, first=(hc == 1))
            pe_dn_done = PE.n
            if e < nexp_run - 1:
                GP.wait_ge(PE.h, pe_dn_done)
                dma_wdn(e + 1)
            if last_e:
                ln2_stats(NCC - 1, first=False)
        ve_ffn_done = VE.n
        if upto.startswith("ffn"):
            SY.wait_ge(VE.h, ve_ffn_done)
            for cc in range(NCC):
                ST.inc(SY.dma_start(
                    out_d.rearrange("(a p) t -> p a t", p=128)[:, cc, :],
                    r_st[:, cc, :]), 16)
            SY.wait_ge(ST.h, ST.n)
            es.close()
            return nc

        ln_rows_bcast()
        for cc in range(NCC):
            ln_apply_chunk(cc, store_out=True, on_gp=(cc % 3 == 2))
        SY.wait_ge(ST.h, ST.n)
        es.close()
    return nc


def _host_prep(inputs):
    x = np.asarray(inputs["L0_attn_x"], np.float32)
    watT = np.ascontiguousarray(np.asarray(inputs["L0_attn_w_attn"], np.float32).T)
    wpT = np.ascontiguousarray(np.asarray(inputs["L0_attn_w_proj"], np.float32).T)
    wshup = np.ascontiguousarray(np.asarray(inputs["L0_ffn_Wsh_up"], np.float32))
    wshdn = np.ascontiguousarray(np.asarray(inputs["L0_ffn_Wsh_down"], np.float32))
    wrtup = np.ascontiguousarray(np.asarray(inputs["L0_ffn_Wrt_up"], np.float32))
    wrtdn = np.ascontiguousarray(np.asarray(inputs["L0_ffn_Wrt_down"], np.float32))
    wr = np.ascontiguousarray(np.asarray(inputs["L0_ffn_W_router"], np.float32))

    trimask = np.where(np.arange(128)[None, :] >= np.arange(128)[:, None],
                       0.0, BIAS_BLOCK).astype(np.float32)
    ident = np.eye(128, dtype=np.float32)
    oh = np.zeros((16, 16 * 128), np.float32)
    for e2 in range(16):
        oh[e2, e2 * 128:(e2 + 1) * 128] = 1.0

    in_maps = []
    owns = []
    for c in range(8):
        b, par = c // 2, c % 2
        if par == 0:
            own = np.r_[np.arange(0, 512), np.arange(1536, 2048)]
            other = np.arange(512, 1536)
        else:
            own = np.arange(512, 1536)
            other = np.r_[np.arange(0, 512), np.arange(1536, 2048)]
        perm = np.r_[other, own]
        owns.append((b, own))
        xT = np.ascontiguousarray(x[b].T)
        xfm = np.ascontiguousarray(xT[:, perm])
        xown = np.ascontiguousarray(xT[:, own])
        bt = np.empty((128, 32), np.float32)
        for qb in range(2):
            qstart = own[512 * qb]
            for s in range(16):
                d = s - (8 + 4 * qb)
                if 0 <= d < 4:
                    bt[:, qb * 16 + s] = BIAS_PASS
                    continue
                kstart = perm[128 * s]
                bt[:, qb * 16 + s] = (BIAS_PASS if kstart + 128 <= qstart
                                      else BIAS_BLOCK)
        in_maps.append({
            "xfm": xfm, "xown": xown, "watT": watT, "wpT": wpT,
            "wshup": wshup, "wshdn": wshdn, "wrtup": wrtup, "wrtdn": wrtdn,
            "wr": wr, "biastab": bt, "trimask": trimask, "ident": ident,
            "oh": oh,
        })
    return in_maps, owns


def kernel(**inputs):
    in_maps, owns = _host_prep(inputs)
    nc = build_program("full")
    res = run_bass_kernel_spmd(nc, in_maps, core_ids=list(range(8)))
    out = np.empty((B, T, C), np.float32)
    for c in range(8):
        b, own = owns[c]
        out[b, own, :] = res.results[c]["out"].T
    return out


# version 11
# speedup vs baseline: 1.0097x; 1.0097x over previous
import numpy as np
import concourse.bass as bass
import concourse.mybir as mybir
from concourse.bass_utils import run_bass_kernel_spmd

F32 = mybir.dt.float32
BF16 = mybir.dt.bfloat16
U8 = mybir.dt.uint8
AF = mybir.ActivationFunctionType
ALU = mybir.AluOpType
AX = mybir.AxisListType.X

B, T, C = 4, 2048, 2048
NH, HS = 16, 128
NCC = C // 128
TOK = 1024
DEXP = 1024
NEXP = 18
SCALE = float(1.0 / np.sqrt(HS))
BIAS_PASS = -4.0
BIAS_BLOCK = -30000.0
LN_EPS = 1e-5

ARENA_BYTES = 193536
OFF_X = 0
OFF_W = 65536
OFF_R = 65536
OFF_KVQ = 114688
OFF_WDN = 131072
OFF_Y = 155648
OFF_MID = 163840
OFF_ROWS = 16384
OFF_G = 180224
OFF_GEL = 184320
OFF_SQ = 180224
OFF_RB = 184320
OFF_X1B = 0
OFF_WUP = 32768
OFF_ATT = 188416
OFF_BC = 191488


class Sem:
    def __init__(self, handle):
        self.h = handle
        self.n = 0

    def inc(self, op, k=1):
        op.then_inc(self.h, k)
        self.n += k
        return self.n


def build_program(upto="full"):
    nc = bass.Bass("TRN2", target_bir_lowering=False)

    def din(name, shape, dtype=F32):
        return nc.dram_tensor(name, shape, dtype, kind="ExternalInput").ap()

    x_fm_d = din("xfm", [C, T])
    x_own_d = din("xown", [C, TOK])
    watT_d = din("watT", [C, 3 * C])
    wpT_d = din("wpT", [C, C])
    wshup_d = din("wshup", [2, C, DEXP])
    wshdn_d = din("wshdn", [2, DEXP, C])
    wrtup_d = din("wrtup", [16, C, DEXP])
    wrtdn_d = din("wrtdn", [16, DEXP, C])
    wr_d = din("wr", [C, 16])
    biastab_d = din("biastab", [128, 32])
    trimask_d = din("trimask", [128, 128])
    ident_d = din("ident", [128, 128])
    oh_d = din("oh", [16, 16 * 128])
    out_d = nc.dram_tensor("out", [C, TOK], F32, kind="ExternalOutput").ap()

    from contextlib import ExitStack
    es = ExitStack()
    arena = es.enter_context(nc.sbuf_tensor("arena", [128, ARENA_BYTES], U8))

    def bfv(off, shape):
        nb = int(np.prod(shape)) * 2
        ap = arena[:, off:off + nb].bitcast(BF16)
        if len(shape) > 1:
            ap = ap.rearrange("p (a b) -> p a b", a=shape[0])
        return ap

    def f32v(off, shape):
        nb = int(np.prod(shape)) * 4
        ap = arena[:, off:off + nb].bitcast(F32)
        if len(shape) > 1:
            ap = ap.rearrange("p (a b) -> p a b", a=shape[0])
        return ap

    x_fm = bfv(OFF_X, [NCC, T])
    wq = [bfv(OFF_W + 24576 * i, [NCC, 256]) for i in range(2)]
    wk = [bfv(OFF_W + 24576 * i + 8192, [NCC, 256]) for i in range(2)]
    wv = [bfv(OFF_W + 24576 * i + 16384, [NCC, 256]) for i in range(2)]
    k_fm = [bfv(OFF_KVQ + 20480 * i, [2, T]) for i in range(2)]
    v_tm = [bfv(OFF_KVQ + 20480 * i + 8192, [16, 256]) for i in range(2)]
    q_fm = [bfv(OFF_KVQ + 20480 * i + 16384, [2, TOK]) for i in range(2)]
    y_fm = bfv(OFF_Y, [NCC, TOK])
    att = [bfv(OFF_ATT + i * 1024, [512]) for i in range(3)]
    bcast_sb = f32v(OFF_BC, [512])
    wp = bfv(OFF_X, [NCC, C])
    r_st = f32v(OFF_R, [NCC, TOK])
    x1b = bfv(OFF_X1B, [NCC, TOK])
    wup = bfv(OFF_WUP, [NCC, DEXP])
    wdn = bfv(OFF_WDN, [8, C])
    mid = bfv(OFF_MID, [8, TOK])
    g_sb = f32v(OFF_G, [TOK])
    gel = [f32v(OFF_GEL + i * 2048, [512]) for i in range(2)]
    rows = f32v(OFF_ROWS, [6144])
    sq_rot1 = [bfv(131072 + i * 2048, [TOK]) for i in range(2)]
    rb_rot1 = [bfv(135168 + i * 2048, [TOK]) for i in range(2)]
    sq_rot2 = [bfv(0 + i * 2048, [TOK]) for i in range(2)]
    rb_rot2 = [bfv(4096 + i * 2048, [TOK]) for i in range(2)]
    rstd_sb = f32v(OFF_SQ, [TOK])
    mrs_sb = f32v(OFF_RB, [TOK])

    if True:
        ec = es.enter_context
        trimask = ec(nc.sbuf_tensor("s_trimask", [128, 128], F32))
        biastab = ec(nc.sbuf_tensor("s_biastab", [128, 32], F32))
        ident = ec(nc.sbuf_tensor("s_ident", [128, 128], F32))
        oh = ec(nc.sbuf_tensor("s_oh", [16, 16 * 128], F32))
        wrsb = ec(nc.sbuf_tensor("s_wrsb", [128, NCC, 16], F32))
        gatefm = ec(nc.sbuf_tensor("s_gatefm", [16, TOK], F32))
        onescb = ec(nc.sbuf_tensor("s_onescb", [128, 1], BF16))
        onesrf = ec(nc.sbuf_tensor("s_onesrf", [1, 128], F32))
        ptm8 = ec(nc.sbuf_tensor("s_ptm8", [128, 8, 16], F32))
        e16_8 = ec(nc.sbuf_tensor("s_e16_8", [128, 8, 16], F32))
        gtm8 = ec(nc.sbuf_tensor("s_gtm8", [128, 8, 16], F32))
        nm8 = ec(nc.sbuf_tensor("s_nm8", [128, 8], F32))
        epsv = ec(nc.sbuf_tensor("s_epsv", [128, 1], F32))
        sm1 = ec(nc.sbuf_tensor("s_sm1", [128, 1], F32))
        m8 = ec(nc.sbuf_tensor("s_m8", [128, 8], F32))
        thr = ec(nc.sbuf_tensor("s_thr", [128, 1], F32))
        srow_t = ec(nc.sbuf_tensor("s_srow", [128, 512], F32))
        WD, SD, PE, AC, VE, ST = (Sem(ec(nc.semaphore(s))) for s in
                                  ("WD", "SD", "PE", "AC", "VE", "ST"))
        XB = [Sem(ec(nc.semaphore(f"XB{i}"))) for i in range(4)]
        WG = [Sem(ec(nc.semaphore(f"WG{i}"))) for i in range(2)]
        WPS = Sem(ec(nc.semaphore("WPS")))
        WUS = Sem(ec(nc.semaphore("WUS")))
        WDS = Sem(ec(nc.semaphore("WDS")))
        SXQ = [Sem(ec(nc.semaphore(f"SXQ{i}"))) for i in range(4)]
        GPC = Sem(ec(nc.semaphore("GPC")))
        WK0 = Sem(ec(nc.semaphore("WK0")))
        WV0 = Sem(ec(nc.semaphore("WV0")))
        WQ0 = Sem(ec(nc.semaphore("WQ0")))
        TE, VEg, SE, GP, SY = nc.tensor, nc.vector, nc.scalar, nc.gpsimd, nc.sync

        pb = [es.enter_context(nc.psum_tensor(f"pb{i}", [128, 512], F32))
              for i in range(8)]

        def load_act(idx):
            SE.add_instruction(mybir.InstLoadActFuncSet(
                name=nc.get_next_instruction_name(),
                act_func_set_id=idx, ins=[], outs=[]))

        load_act(6)
        VEg.memset(onescb[:], 1.0)
        VEg.memset(epsv[:], LN_EPS)
        VE.inc(VEg.memset(onesrf[:], 1.0))
        ve_memset = VE.n

        SD.inc(SY.dma_start(trimask[:], trimask_d), 16)
        SD.inc(SY.dma_start(biastab[:], biastab_d), 16)
        SD.inc(SY.dma_start(ident[:], ident_d), 16)
        SD.inc(SY.dma_start(oh[:], oh_d), 16)
        SD.inc(SY.dma_start(
            wrsb[:, :, :],
            wr_d.rearrange("(a p) b -> p a b", p=128)), 16)
        sd_consts = SD.n
        VEg.wait_ge(SD.h, sd_consts)
        SE.wait_ge(SD.h, sd_consts)
        TE.wait_ge(VE.h, ve_memset)

        wd_wg = []
        wd_xb = [0] * 4

        def dma_wgroup(g):
            buf = g % 2
            for dst, base in ((wq[buf], 0), (wk[buf], C), (wv[buf], 2 * C)):
                WG[buf].inc(GP.dma_start(
                    dst[:, :, :],
                    watT_d.rearrange("(a p) o -> p a o", p=128)
                    [:, :, base + 256 * g: base + 256 * g + 256]), 16)
            wd_wg.append(WG[buf].n)

        def dma_x(xb):
            XB[xb].inc(GP.dma_start(
                x_fm[:, :, xb * 512:xb * 512 + 512],
                x_fm_d.rearrange("(a p) t -> p a t", p=128)
                [:, :, xb * 512:xb * 512 + 512]), 16)

        watT_r = watT_d.rearrange("(a p) o -> p a o", p=128)
        WK0.inc(GP.dma_start(wk[0][:, :, :], watT_r[:, :, C:C + 256]), 16)
        dma_x(0)
        WV0.inc(GP.dma_start(wv[0][:, :, :], watT_r[:, :, 2 * C:2 * C + 256]), 16)
        dma_x(1)
        WQ0.inc(GP.dma_start(wq[0][:, :, :], watT_r[:, :, 0:256]), 16)
        dma_x(2)
        dma_x(3)
        wd_wg.append(0)
        dma_wgroup_early = [False]

        NG = 8
        pe_qkv_done = [0] * NG
        pe_att_done = [0] * NG
        ac_att_done = [0] * NG
        bank_free_ve = {}
        exp_ac_of_bank = {0: 0, 1: 0, 2: 0}
        sumav_pe_of_att = {0: 0, 1: 0, 2: 0}
        sig_ve_free = {3: 0, 4: 0}
        av_ve_free = {5: 0, 6: 0}
        bc_ve_free = [0]
        deferred = []
        srow_pe_guard = [0]
        qkv_ti = [0]

        def qkv_group(g):
            buf = g % 2
            if g == 0:
                TE.wait_ge(WK0.h, 16)
            else:
                TE.wait_ge(WG[g % 2].h, wd_wg[g])
            if g >= 2:
                TE.wait_ge(AC.h, AC.n)
            if g == 0:
                TE.wait_ge(WV0.h, 16) if False else None
            xwait = set()

            def one_tile(lhs_fn, rhs_fn, evac_to, need_xb, n_k=NCC):
                if need_xb not in xwait:
                    TE.wait_ge(XB[need_xb].h, 16)
                    xwait.add(need_xb)
                bank = qkv_ti[0] % 3
                if bank_free_ve.get(bank, 0) > 0:
                    TE.wait_ge(VE.h, bank_free_ve[bank])
                w = evac_to.shape[-1]
                for ccx in range(n_k):
                    op = TE.matmul(pb[bank][:, 0:w], lhs_fn(ccx), rhs_fn(ccx),
                                   start=(ccx == 0), stop=(ccx == n_k - 1))
                PE.inc(op)
                VEg.wait_ge(PE.h, PE.n)
                VE.inc(VEg.tensor_copy(evac_to, pb[bank][:, 0:w]))
                bank_free_ve[bank] = VE.n
                qkv_ti[0] += 1

            for tb in range(4):
                for hr in range(2):
                    one_tile(lambda cc, hr=hr: wk[buf][:, cc, hr * 128:hr * 128 + 128],
                             lambda cc, tb=tb: x_fm[:, cc, tb * 512:tb * 512 + 512],
                             k_fm[buf][:, hr, tb * 512:tb * 512 + 512], tb)
                if g == 0 and tb == 0:
                    TE.wait_ge(WV0.h, 16)
                for tc in range(4 * tb, 4 * tb + 4):
                    one_tile(lambda cc, tc=tc: x_fm[:, cc, tc * 128:tc * 128 + 128],
                             lambda cc: wv[buf][:, cc, 0:256],
                             v_tm[buf][:, tc, 0:256], tb)
            if g == 0:
                TE.wait_ge(WQ0.h, 16)
            for hr in range(2):
                for tb in range(2):
                    one_tile(lambda cc, hr=hr: wq[buf][:, cc, hr * 128:hr * 128 + 128],
                             lambda cc, tb=tb: x_fm[:, cc, 1024 + tb * 512:1536 + tb * 512],
                             q_fm[buf][:, hr, tb * 512:tb * 512 + 512], 2 + tb)
            pe_qkv_done[g] = PE.n
            if g < NG - 1:
                if g + 1 == 1:
                    dma_wgroup(1)
                else:
                    GP.wait_ge(PE.h, pe_qkv_done[g])
                    dma_wgroup(g + 1)

        def att_group(g):
            buf = g % 2
            TE.wait_ge(VE.h, VE.n)
            tasks = []
            for hr in range(2):
                for qb in range(2):
                    slots = [0, 1, 2, 3, 8, 9, 10, 11] if qb == 0 else list(range(16))
                    for si, s in enumerate(slots):
                        tasks.append((hr, qb, s, si == 0, si == len(slots) - 1))
            L = len(tasks)

            def sc_emit(j):
                hr, qb, s, first, last = tasks[j]
                d = s - (8 + 4 * qb)
                diag = 0 <= d < 4
                coloff = 128 * d if diag else 0
                N = 512 - coloff
                bank = j % 3
                if exp_ac_of_bank[bank] > 0:
                    TE.wait_ge(AC.h, exp_ac_of_bank[bank])
                PE.inc(TE.matmul(pb[bank][:, coloff:coloff + N],
                                 k_fm[buf][:, hr, s * 128:s * 128 + 128],
                                 q_fm[buf][:, hr, qb * 512 + coloff:qb * 512 + coloff + N],
                                 start=True, stop=True))
                pe_sc = PE.n
                if diag:
                    VEg.wait_ge(PE.h, pe_sc)
                    VE.inc(VEg.tensor_tensor(pb[bank][:, coloff:coloff + 128],
                                             pb[bank][:, coloff:coloff + 128],
                                             trimask[:], ALU.add))
                    SE.wait_ge(VE.h, VE.n)
                else:
                    SE.wait_ge(PE.h, pe_sc)
                ab = j % 3
                if sumav_pe_of_att[ab] > 0:
                    SE.wait_ge(PE.h, sumav_pe_of_att[ab])
                AC.inc(SE.activation(att[ab][:, coloff:coloff + N],
                                     pb[bank][:, coloff:coloff + N],
                                     AF.Exp,
                                     bias=biastab[:, qb * 16 + s:qb * 16 + s + 1],
                                     scale=SCALE))
                exp_ac_of_bank[bank] = AC.n
                return AC.n

            def sumav_emit(j, ac_exp):
                hr, qb, s, first, last = tasks[j]
                d = s - (8 + 4 * qb)
                diag = 0 <= d < 4
                coloff = 128 * d if diag else 0
                N = 512 - coloff
                i2 = (g * 2 + hr) * 2 + qb
                sig_b = 3 + (i2 % 2)
                av_b = 5 + (i2 % 2)
                ab = j % 3
                TE.wait_ge(AC.h, ac_exp)
                if first:
                    if sig_ve_free[sig_b] > 0:
                        TE.wait_ge(VE.h, sig_ve_free[sig_b])
                    if av_ve_free[av_b] > 0:
                        TE.wait_ge(VE.h, av_ve_free[av_b])
                TE.matmul(pb[sig_b][0:1, coloff:coloff + N], onescb[:],
                          att[ab][:, coloff:coloff + N], start=first, stop=last)
                PE.inc(TE.matmul(pb[av_b][:, coloff:coloff + N],
                                 v_tm[buf][:, s, hr * 128:hr * 128 + 128],
                                 att[ab][:, coloff:coloff + N],
                                 start=first, stop=last))
                sumav_pe_of_att[ab] = PE.n
                if last:
                    pe_done = PE.n
                    VEg.wait_ge(PE.h, max(pe_done, srow_pe_guard[0]))
                    VE.inc(VEg.reciprocal(srow_t[0:1, 0:512],
                                          pb[sig_b][0:1, 0:512]))
                    sig_ve_free[sig_b] = VE.n
                    ve_recip = VE.n

                    def fin(sig_b=sig_b, av_b=av_b, hr=hr, qb=qb,
                            ve_recip=ve_recip):
                        TE.wait_ge(VE.h, ve_recip)
                        if bc_ve_free[0] > 0:
                            TE.wait_ge(VE.h, bc_ve_free[0])
                        PE.inc(TE.matmul(pb[7][:, 0:512], onesrf[:],
                                         srow_t[0:1, 0:512],
                                         start=True, stop=True))
                        VEg.wait_ge(PE.h, PE.n)
                        VE.inc(VEg.tensor_copy(bcast_sb[:, 0:512],
                                               pb[7][:, 0:512]))
                        bc_ve_free[0] = VE.n
                        VEg.wait_ge(VE.h, VE.n)
                        VE.inc(VEg.tensor_tensor(
                            y_fm[:, g * 2 + hr, qb * 512:qb * 512 + 512],
                            pb[av_b][:, 0:512], bcast_sb[:, 0:512], ALU.mult))
                        av_ve_free[av_b] = VE.n
                    deferred.append(fin)

            LA = 3
            pend = []
            fin_delay = [0]
            for j in range(L):
                if deferred:
                    fin_delay[0] += 1
                    if fin_delay[0] >= 4:
                        deferred.pop(0)()
                        srow_pe_guard[0] = PE.n
                        fin_delay[0] = 0
                if len(pend) == LA:
                    jj, ac = pend.pop(0)
                    sumav_emit(jj, ac)
                pend.append((j, sc_emit(j)))

            def drain():
                while pend:
                    jj, ac = pend.pop(0)
                    sumav_emit(jj, ac)
                pe_att_done[g] = PE.n
                ac_att_done[g] = AC.n
            return drain

        qkv_group(0)
        qkv_group(1)
        for g in range(NG):
            dr = att_group(g)
            dr()
            if g + 2 < NG:
                qkv_group(g + 2)
        while deferred:
            deferred.pop(0)()
            srow_pe_guard[0] = PE.n

        GP.wait_ge(PE.h, pe_qkv_done[NG - 1])
        WPS.inc(GP.dma_start(wp[:, :, :],
                             wpT_d.rearrange("(a p) o -> p a o", p=128)), 16)
        for q4 in range(4):
            SY.wait_ge(PE.h, pe_qkv_done[NG - 1] if q4 < 3
                       else pe_att_done[NG - 2])
            SXQ[q4].inc(SY.dma_start(
                r_st[:, q4 * 4:q4 * 4 + 4, :],
                x_own_d.rearrange("(a p) t -> p a t", p=128)[:, q4 * 4:q4 * 4 + 4, :]), 16)

        apply_ve_gate = [0]

        def mk_ln_stats(rb_rot, sq_rot):
            pe_s_free = {0: 0, 1: 0}

            def ln_stats_chunk(cc, first):
                par = cc % 2
                VEg.wait_ge(VE.h, VE.n)
                if pe_s_free[par] > 0:
                    VEg.wait_ge(PE.h, pe_s_free[par])
                VE.inc(VEg.tensor_copy(rb_rot[par][:, 0:TOK], r_st[:, cc, :]))
                VEg.wait_ge(VE.h, VE.n)
                VE.inc(VEg.tensor_tensor(sq_rot[par][:, 0:TOK],
                                         rb_rot[par][:, 0:TOK],
                                         rb_rot[par][:, 0:TOK], ALU.mult))
                ve_c = VE.n
                TE.wait_ge(VE.h, ve_c)
                for tb in range(2):
                    TE.matmul(pb[4 + tb][0:1, 0:512], onescb[:],
                              rb_rot[par][:, tb * 512:tb * 512 + 512],
                              start=first, stop=(cc == NCC - 1))
                    op = TE.matmul(pb[6 + tb][0:1, 0:512], onescb[:],
                                   sq_rot[par][:, tb * 512:tb * 512 + 512],
                                   start=first, stop=(cc == NCC - 1))
                PE.inc(op)
                pe_s_free[par] = PE.n
            return ln_stats_chunk

        def ln_rows_bcast():
            row0 = rows[0:1, 0:1024]
            row1 = rows[0:1, 1024:2048]
            row2 = rows[0:1, 2048:3072]
            VEg.wait_ge(PE.h, PE.n)
            VEg.tensor_copy(row0[0:1, 0:512], pb[4][0:1, 0:512])
            VEg.tensor_copy(row0[0:1, 512:1024], pb[5][0:1, 0:512])
            VEg.tensor_copy(row1[0:1, 0:512], pb[6][0:1, 0:512])
            VE.inc(VEg.tensor_copy(row1[0:1, 512:1024], pb[7][0:1, 0:512]))
            VEg.wait_ge(VE.h, VE.n)
            VE.inc(VEg.tensor_scalar(row0, row0, 1.0 / C, None, ALU.mult))
            VEg.wait_ge(VE.h, VE.n)
            VE.inc(VEg.tensor_tensor(row2, row0, row0, ALU.mult))
            VEg.wait_ge(VE.h, VE.n)
            VE.inc(VEg.scalar_tensor_tensor(row1, row1, 1.0 / C, row2,
                                            ALU.mult, ALU.subtract))
            ve_rows = VE.n
            load_act(6)
            SE.wait_ge(VE.h, ve_rows)
            AC.inc(SE.activation(row1, row1, AF.Ln, bias=epsv[0:1, 0:1]))
            SE.wait_ge(AC.h, AC.n)
            AC.inc(SE.activation(row1, row1, AF.Exp, scale=-0.5))
            ac_rstd = AC.n
            VEg.wait_ge(AC.h, ac_rstd)
            VE.inc(VEg.tensor_tensor(row2, row0, row1, ALU.mult))
            ve_mrs = VE.n
            TE.wait_ge(VE.h, ve_mrs)
            for tb in range(2):
                TE.matmul(pb[0 + tb][:, 0:512], onesrf[:],
                          row1[0:1, tb * 512:tb * 512 + 512], start=True, stop=True)
                op = TE.matmul(pb[2 + tb][:, 0:512], onesrf[:],
                               row2[0:1, tb * 512:tb * 512 + 512], start=True, stop=True)
            PE.inc(op)
            VEg.wait_ge(PE.h, PE.n)
            VEg.tensor_copy(rstd_sb[:, 0:512], pb[0][:, 0:512])
            VEg.tensor_copy(rstd_sb[:, 512:1024], pb[1][:, 0:512])
            VEg.tensor_copy(mrs_sb[:, 0:512], pb[2][:, 0:512])
            VE.inc(VEg.tensor_copy(mrs_sb[:, 512:1024], pb[3][:, 0:512]))
            apply_ve_gate[0] = VE.n
            VEg.wait_ge(VE.h, VE.n)

        def ln_apply_chunk(cc, cast_x1=False, store_out=False, router=False,
                           on_gp=False):
            if on_gp:
                GP.wait_ge(VE.h, apply_ve_gate[0])
                GPC.inc(GP.tensor_tensor(r_st[:, cc, :], r_st[:, cc, :],
                                         rstd_sb[:, 0:TOK], ALU.mult))
                GP.wait_ge(GPC.h, GPC.n)
                GPC.inc(GP.tensor_tensor(r_st[:, cc, :], r_st[:, cc, :],
                                         mrs_sb[:, 0:TOK], ALU.subtract))
                gp_sub = GPC.n
                if cast_x1:
                    VEg.wait_ge(GPC.h, gp_sub)
                    VE.inc(VEg.tensor_copy(x1b[:, cc, :], r_st[:, cc, :]))
                if router:
                    TE.wait_ge(GPC.h, gp_sub)
                    for tc in range(8):
                        op = TE.matmul(pb[6][:, tc * 16:tc * 16 + 16],
                                       r_st[:, cc, tc * 128:tc * 128 + 128],
                                       wrsb[:, cc, :],
                                       start=(cc == 0 and tc == 0),
                                       stop=(cc == NCC - 1 and tc == 7))
                    PE.inc(op)
                if store_out:
                    SY.wait_ge(GPC.h, gp_sub)
                    ST.inc(SY.dma_start(
                        out_d.rearrange("(a p) t -> p a t", p=128)[:, cc, :],
                        r_st[:, cc, :]), 16)
                return
            VE.inc(VEg.tensor_tensor(r_st[:, cc, :], r_st[:, cc, :],
                                     rstd_sb[:, 0:TOK], ALU.mult))
            VEg.wait_ge(VE.h, VE.n)
            VE.inc(VEg.tensor_tensor(r_st[:, cc, :], r_st[:, cc, :],
                                     mrs_sb[:, 0:TOK], ALU.subtract))
            ve_sub = VE.n
            VEg.wait_ge(VE.h, ve_sub)
            if cast_x1:
                VE.inc(VEg.tensor_copy(x1b[:, cc, :], r_st[:, cc, :]))
            if router:
                TE.wait_ge(VE.h, ve_sub)
                for tc in range(8):
                    op = TE.matmul(pb[6][:, tc * 16:tc * 16 + 16],
                                   r_st[:, cc, tc * 128:tc * 128 + 128],
                                   wrsb[:, cc, :],
                                   start=(cc == 0 and tc == 0),
                                   stop=(cc == NCC - 1 and tc == 7))
                PE.inc(op)
            if store_out:
                SY.wait_ge(VE.h, ve_sub)
                ST.inc(SY.dma_start(
                    out_d.rearrange("(a p) t -> p a t", p=128)[:, cc, :],
                    r_st[:, cc, :]), 16)

        TE.wait_ge(WPS.h, 16)
        TE.wait_ge(AC.h, AC.n)
        TE.wait_ge(VE.h, VE.n)
        proj_free = {0: 0, 1: 0}
        xq_waited = set()
        for oc in range(NCC):
            for tb in range(2):
                bank = (oc * 2 + tb) % 2
                if proj_free[bank] > 0:
                    TE.wait_ge(VE.h, proj_free[bank])
                for cc in range(NCC):
                    op = TE.matmul(pb[bank][:, 0:512],
                                   wp[:, cc, oc * 128:oc * 128 + 128],
                                   y_fm[:, cc, tb * 512:tb * 512 + 512],
                                   start=(cc == 0), stop=(cc == NCC - 1))
                PE.inc(op)
                VEg.wait_ge(PE.h, PE.n)
                if (oc // 4) not in xq_waited:
                    VEg.wait_ge(SXQ[oc // 4].h, 16)
                    xq_waited.add(oc // 4)
                VE.inc(VEg.tensor_tensor(r_st[:, oc, tb * 512:tb * 512 + 512],
                                         pb[bank][:, 0:512],
                                         r_st[:, oc, tb * 512:tb * 512 + 512],
                                         ALU.add))
                proj_free[bank] = VE.n
            if oc == 0:
                ln1_stats = mk_ln_stats(rb_rot1, sq_rot1)
            else:
                ln1_stats(oc - 1, first=(oc == 1))
        ln1_stats(NCC - 1, first=False)
        ve_proj_done = VE.n
        pe_proj_done = PE.n

        if upto == "y":
            GP.wait_ge(VE.h, VE.n)
            for cc in range(NCC):
                ST.inc(GP.dma_start(
                    out_d.rearrange("(a p) t -> p a t", p=128)[:, cc, :],
                    y_fm[:, cc, :]), 16)
            GP.wait_ge(ST.h, ST.n)
            es.close()
            return nc
        if upto == "resid":
            SY.wait_ge(VE.h, ve_proj_done)
            for cc in range(NCC):
                ST.inc(SY.dma_start(
                    out_d.rearrange("(a p) t -> p a t", p=128)[:, cc, :],
                    r_st[:, cc, :]), 16)
            SY.wait_ge(ST.h, ST.n)
            es.close()
            return nc

        GP.wait_ge(PE.h, pe_proj_done)
        wd_wup = [0] * NEXP
        wd_wdn = [0] * NEXP

        def dma_wup(e):
            srcw = wshup_d[e] if e < 2 else wrtup_d[e - 2]
            WUS.inc(GP.dma_start(wup[:, :, :],
                                 srcw.rearrange("(a p) d -> p a d", p=128)), 16)
            wd_wup[e] = WUS.n

        def dma_wdn(e):
            srcw = wshdn_d[e] if e < 2 else wrtdn_d[e - 2]
            WDS.inc(GP.dma_start(wdn[:, :, :],
                                 srcw.rearrange("(a p) h -> p a h", p=128)), 16)
            wd_wdn[e] = WDS.n

        dma_wup(0)
        dma_wdn(0)

        ln_rows_bcast()
        for cc in range(NCC):
            ln_apply_chunk(cc, cast_x1=True, router=True,
                           store_out=(upto == "x1"))
        ve_ln1 = VE.n
        pe_router = PE.n
        if upto == "x1":
            SY.wait_ge(ST.h, ST.n)
            es.close()
            return nc

        VEg.wait_ge(PE.h, pe_router)
        for tc in range(8):
            VE.inc(VEg.tensor_copy(ptm8[:, tc, :], pb[6][:, tc * 16:tc * 16 + 16]))
        VEg.wait_ge(VE.h, VE.n)
        for tc in range(8):
            VE.inc(VEg.reduce_max(nm8[:, tc:tc + 1], ptm8[:, tc, :], axis=AX,
                                  negate=True))
        ve_v1 = VE.n
        SE.wait_ge(VE.h, ve_v1)
        for tc in range(8):
            AC.inc(SE.activation(e16_8[:, tc, :], ptm8[:, tc, :], AF.Exp,
                                 bias=nm8[:, tc:tc + 1]))
        ac_exp8 = AC.n
        load_act(10)
        VEg.wait_ge(AC.h, ac_exp8)
        for tc in range(8):
            VE.inc(VEg.reduce_sum(sm1[:], e16_8[:, tc, :], axis=AX))
            VEg.wait_ge(VE.h, VE.n)
            VE.inc(VEg.reciprocal(sm1[:], sm1[:]))
            VEg.wait_ge(VE.h, VE.n)
            VE.inc(VEg.tensor_scalar(ptm8[:, tc, :], e16_8[:, tc, :], sm1[:],
                                     None, ALU.mult))
            VEg.wait_ge(VE.h, VE.n)
            VE.inc(VEg.max(m8[:], ptm8[:, tc, :]))
            VEg.wait_ge(VE.h, VE.n)
            VE.inc(VEg.tensor_copy(thr[:], m8[:, 1:2]))
            VEg.wait_ge(VE.h, VE.n)
            VE.inc(VEg.scalar_tensor_tensor(gtm8[:, tc, :], ptm8[:, tc, :],
                                            thr[:], ptm8[:, tc, :],
                                            ALU.is_ge, ALU.mult))
            VEg.wait_ge(VE.h, VE.n)
        ve_chains = VE.n

        nexp_run = NEXP
        if upto.startswith("ffn") and len(upto) > 3:
            nexp_run = int(upto[3:])
        gel_ve_free = {0: 0, 1: 0}
        up_ac_free = {0: 0, 1: 0}
        dn_ve_free = {2: 0, 3: 0}
        g_ve_free = [0]
        ve_gate_done = [0]
        gate_emitted = [False]

        def emit_gate_transposes():
            TE.wait_ge(VE.h, ve_chains)
            for tc in range(8):
                gb = 2 + (tc // 4)
                if dn_ve_free[gb] > 0:
                    TE.wait_ge(VE.h, dn_ve_free[gb])
                PE.inc(TE.transpose(pb[gb][0:16, (tc % 4) * 128:(tc % 4) * 128 + 128],
                                    gtm8[:, tc, :], ident[:]))
                if tc % 4 == 3:
                    VEg.wait_ge(PE.h, PE.n)
                    VE.inc(VEg.tensor_copy(
                        gatefm[0:16, (tc // 4) * 512:(tc // 4) * 512 + 512],
                        pb[gb][0:16, 0:512]))
            dn_ve_free[2] = VE.n
            dn_ve_free[3] = VE.n
            ve_gate_done[0] = VE.n
            gate_emitted[0] = True

        if upto == "gate":
            emit_gate_transposes()
            SY.wait_ge(VE.h, VE.n)
            ST.inc(SY.dma_start(out_d[0:16, :], gatefm[0:16, 0:TOK]), 16)
            SY.wait_ge(ST.h, ST.n)
            es.close()
            return nc

        for e in range(nexp_run):
            routed = e >= 2
            ve_gready = 0
            if routed:
                if not gate_emitted[0]:
                    emit_gate_transposes()
                re = e - 2
                TE.wait_ge(VE.h, max(ve_gate_done[0], g_ve_free[0]))
                for tb in range(2):
                    op = TE.matmul(pb[4 + tb][:, 0:512],
                                   oh[:, re * 128:re * 128 + 128],
                                   gatefm[0:16, tb * 512:tb * 512 + 512],
                                   start=True, stop=True)
                PE.inc(op)
                VEg.wait_ge(PE.h, PE.n)
                VEg.tensor_copy(g_sb[:, 0:512], pb[4][:, 0:512])
                VE.inc(VEg.tensor_copy(g_sb[:, 512:1024], pb[5][:, 0:512]))
                g_ve_free[0] = VE.n
                ve_gready = VE.n
            TE.wait_ge(WUS.h, wd_wup[e])
            mid_ready_tb = [0, 0]
            for tb in range(2):
                for dc in range(8):
                    bank = dc % 2
                    if up_ac_free[bank] > 0:
                        TE.wait_ge(AC.h, up_ac_free[bank])
                    for hc in range(NCC):
                        op = TE.matmul(pb[bank][:, 0:512],
                                       wup[:, hc, dc * 128:dc * 128 + 128],
                                       x1b[:, hc, tb * 512:tb * 512 + 512],
                                       start=(hc == 0), stop=(hc == NCC - 1))
                    PE.inc(op)
                    SE.wait_ge(PE.h, PE.n)
                    if routed:
                        gb2 = dc % 2
                        if gel_ve_free[gb2] > 0:
                            SE.wait_ge(VE.h, gel_ve_free[gb2])
                        AC.inc(SE.activation(gel[gb2][:, 0:512], pb[bank][:, 0:512],
                                             AF.Gelu))
                        up_ac_free[bank] = AC.n
                        VEg.wait_ge(AC.h, AC.n)
                        if dc == 0 and tb == 0:
                            VEg.wait_ge(VE.h, ve_gready)
                        VE.inc(VEg.tensor_tensor(
                            mid[:, dc, tb * 512:tb * 512 + 512],
                            gel[gb2][:, 0:512],
                            g_sb[:, tb * 512:tb * 512 + 512], ALU.mult))
                        gel_ve_free[gb2] = VE.n
                        mid_ready_tb[tb] = VE.n
                    else:
                        AC.inc(SE.activation(mid[:, dc, tb * 512:tb * 512 + 512],
                                             pb[bank][:, 0:512], AF.Gelu))
                        up_ac_free[bank] = AC.n
                        mid_ready_tb[tb] = AC.n
            pe_up_done = PE.n
            if e < nexp_run - 1:
                GP.wait_ge(PE.h, pe_up_done)
                dma_wup(e + 1)
            TE.wait_ge(WDS.h, wd_wdn[e])
            last_e = (e == nexp_run - 1)
            for tb in range(2):
                if routed:
                    TE.wait_ge(VE.h, mid_ready_tb[tb])
                else:
                    TE.wait_ge(AC.h, mid_ready_tb[tb])
                for hc in range(NCC):
                    bank = 2 + hc % 2
                    if dn_ve_free[bank] > 0:
                        TE.wait_ge(VE.h, dn_ve_free[bank])
                    for dc in range(8):
                        op = TE.matmul(pb[bank][:, 0:512],
                                       wdn[:, dc, hc * 128:hc * 128 + 128],
                                       mid[:, dc, tb * 512:tb * 512 + 512],
                                       start=(dc == 0), stop=(dc == 7))
                    PE.inc(op)
                    VEg.wait_ge(PE.h, PE.n)
                    VE.inc(VEg.tensor_tensor(
                        r_st[:, hc, tb * 512:tb * 512 + 512],
                        pb[bank][:, 0:512],
                        r_st[:, hc, tb * 512:tb * 512 + 512], ALU.add))
                    dn_ve_free[bank] = VE.n
                    if last_e and tb == 1:
                        if hc == 0:
                            ln2_stats = mk_ln_stats(rb_rot2, sq_rot2)
                        elif hc >= 2:
                            ln2_stats(hc - 2, first=(hc == 2))
            pe_dn_done = PE.n
            if e < nexp_run - 1:
                GP.wait_ge(PE.h, pe_dn_done)
                dma_wdn(e + 1)
            if last_e:
                ln2_stats(NCC - 2, first=False)
                ln2_stats(NCC - 1, first=False)
        ve_ffn_done = VE.n
        if upto.startswith("ffn"):
            SY.wait_ge(VE.h, ve_ffn_done)
            for cc in range(NCC):
                ST.inc(SY.dma_start(
                    out_d.rearrange("(a p) t -> p a t", p=128)[:, cc, :],
                    r_st[:, cc, :]), 16)
            SY.wait_ge(ST.h, ST.n)
            es.close()
            return nc

        ln_rows_bcast()
        for cc in range(NCC):
            ln_apply_chunk(cc, store_out=True)
        SY.wait_ge(ST.h, ST.n)
        es.close()
    return nc


def _host_prep(inputs):
    x = np.asarray(inputs["L0_attn_x"], np.float32)
    watT = np.ascontiguousarray(np.asarray(inputs["L0_attn_w_attn"], np.float32).T)
    wpT = np.ascontiguousarray(np.asarray(inputs["L0_attn_w_proj"], np.float32).T)
    wshup = np.ascontiguousarray(np.asarray(inputs["L0_ffn_Wsh_up"], np.float32))
    wshdn = np.ascontiguousarray(np.asarray(inputs["L0_ffn_Wsh_down"], np.float32))
    wrtup = np.ascontiguousarray(np.asarray(inputs["L0_ffn_Wrt_up"], np.float32))
    wrtdn = np.ascontiguousarray(np.asarray(inputs["L0_ffn_Wrt_down"], np.float32))
    wr = np.ascontiguousarray(np.asarray(inputs["L0_ffn_W_router"], np.float32))

    trimask = np.where(np.arange(128)[None, :] >= np.arange(128)[:, None],
                       0.0, BIAS_BLOCK).astype(np.float32)
    ident = np.eye(128, dtype=np.float32)
    oh = np.zeros((16, 16 * 128), np.float32)
    for e2 in range(16):
        oh[e2, e2 * 128:(e2 + 1) * 128] = 1.0

    in_maps = []
    owns = []
    for c in range(8):
        b, par = c // 2, c % 2
        if par == 0:
            own = np.r_[np.arange(0, 512), np.arange(1536, 2048)]
            other = np.arange(512, 1536)
        else:
            own = np.arange(512, 1536)
            other = np.r_[np.arange(0, 512), np.arange(1536, 2048)]
        perm = np.r_[other, own]
        owns.append((b, own))
        xT = np.ascontiguousarray(x[b].T)
        xfm = np.ascontiguousarray(xT[:, perm])
        xown = np.ascontiguousarray(xT[:, own])
        bt = np.empty((128, 32), np.float32)
        for qb in range(2):
            qstart = own[512 * qb]
            for s in range(16):
                d = s - (8 + 4 * qb)
                if 0 <= d < 4:
                    bt[:, qb * 16 + s] = BIAS_PASS
                    continue
                kstart = perm[128 * s]
                bt[:, qb * 16 + s] = (BIAS_PASS if kstart + 128 <= qstart
                                      else BIAS_BLOCK)
        in_maps.append({
            "xfm": xfm, "xown": xown, "watT": watT, "wpT": wpT,
            "wshup": wshup, "wshdn": wshdn, "wrtup": wrtup, "wrtdn": wrtdn,
            "wr": wr, "biastab": bt, "trimask": trimask, "ident": ident,
            "oh": oh,
        })
    return in_maps, owns


def kernel(**inputs):
    in_maps, owns = _host_prep(inputs)
    nc = build_program("full")
    res = run_bass_kernel_spmd(nc, in_maps, core_ids=list(range(8)))
    out = np.empty((B, T, C), np.float32)
    for c in range(8):
        b, own = owns[c]
        out[b, own, :] = res.results[c]["out"].T
    return out
